# version 03 (final)
# baseline (speedup 1.0000x reference)
import numpy as np
import concourse.bass as bass
import concourse.mybir as mybir
from concourse.bass_utils import run_bass_kernel_spmd

F32 = mybir.dt.float32
BF16 = mybir.dt.bfloat16
AF = mybir.ActivationFunctionType
ALU = mybir.AluOpType

B, S, D = 4, 4096, 1024
T = 2048
HALO = 258
XW = HALO + T
LA, HA, DHA, FA = 4, 8, 128, 2048
LB, HB, DHB, WIN = 6, 16, 64, 256
DI, DIP = 2730, 2816
QK_SCALE = 8.0
EPS = 1e-5
CH = 128
NTB = T // CH
NST = 4
NEG = -1.0e30

ENGINES = ("tensor", "vector", "scalar", "gpsimd", "sync")


def host_tables():
    F64 = np.float64
    t = {}
    gam = 1.0 - np.exp(np.linspace(np.log(1 / 32), np.log(1 / 512), HA, dtype=F64))
    pos = np.arange(S, dtype=F64)
    invf = 1.0 / 10000.0 ** (np.arange(0, DHA, 2, dtype=F64) / DHA)
    ang = pos[:, None] * invf[None, :]
    t["cosA"] = np.cos(ang).astype(np.float32)
    t["sinA"] = np.sin(ang).astype(np.float32)
    ni = np.arange(CH, dtype=F64)
    diff = ni[None, :] - ni[:, None]
    t["decayD"] = np.where(diff >= 0, gam[:, None, None] ** diff[None], 0.0).astype(np.float32)
    gq = gam[:, None] ** (ni[None, :] + 1)
    t["gqT"] = np.broadcast_to(gq[:, None, :].astype(np.float32), (HA, 128, CH)).copy()
    t["gkcol"] = (gam[:, None] ** (CH - 1 - ni[None, :])).astype(np.float32).T.copy()
    t["gc"] = (gam ** CH).astype(np.float32)
    half = np.arange(0, DHB, 2, dtype=F64)
    invb = 1.0 / 10000.0 ** (half / DHB)
    tt = np.arange(S, dtype=F64)
    fr = tt[:, None] * invb[None, :]
    base = (half + 0.4 * DHB) / (1.4 * DHB)
    power = (tt - S // 2) / (WIN // 2)
    sc = base[None, :] ** power[:, None]
    cosb, sinb = np.cos(fr), np.sin(fr)

    def cat(a):
        return np.concatenate([a, a], -1)
    t["csq"] = (cat(cosb * sc) * QK_SCALE).astype(np.float32)
    ssq = cat(sinb * sc) * QK_SCALE
    ssq[:, :DHB // 2] *= -1.0
    t["ssq"] = ssq.astype(np.float32)
    t["csk"] = cat(cosb / sc).astype(np.float32)
    ssk = cat(sinb / sc)
    ssk[:, :DHB // 2] *= -1.0
    t["ssk"] = ssk.astype(np.float32)
    t["identity"] = np.eye(128, dtype=np.float32)
    t["ones"] = np.ones((128, 128), np.float32)
    return t


def stage_b_masks(core):
    il = np.arange(128)[:, None]
    jl = np.arange(384)[None, :]
    allowed = (jl >= il) & (jl <= il + 256)
    out = {"mask_int": np.where(allowed, 0.0, NEG).astype(np.float32)}
    for qb in (0, 1):
        a = allowed.copy()
        if core % 2 == 0:
            a &= (qb * 128 + jl - 256 >= 0)
        out[f"mask_e{qb}"] = np.where(a, 0.0, NEG).astype(np.float32)
    return out


class Buf:
    __slots__ = ("name", "w", "readers")

    def __init__(self, name):
        self.name = name
        self.w = None
        self.readers = []


class Emit:
    def __init__(self, nc):
        self.nc = nc
        self.cmds = {e: [] for e in ENGINES}
        self.sems = {}
        self.cnt = {}
        self.waited = {e: {} for e in ENGINES}
        self.bufs = {}
        self._keep = []
        self.dma_rr = {}

    def buf(self, name):
        if name not in self.bufs:
            self.bufs[name] = Buf(name)
        return self.bufs[name]

    def sem_name(self, eng, dma):
        return ("d_" + eng) if dma else ("c_" + eng)

    def _need(self, eng, sem, val):
        if sem == self.sem_name(eng, False):
            return self.waited[eng].get(sem, 0) < val
        return self.waited[eng].get(sem, 0) < val

    DMA_POOL = 6

    def do(self, eng, fn, reads=(), writes=(), dma=False, n=1):
        waits = {}
        for b in reads:
            bb = self.buf(b)
            if bb.w:
                s, v = bb.w
                if self._need(eng, s, v):
                    waits[s] = max(waits.get(s, 0), v)
        for b in writes:
            bb = self.buf(b)
            if bb.w:
                s, v = bb.w
                if self._need(eng, s, v):
                    waits[s] = max(waits.get(s, 0), v)
            for (s, v) in bb.readers:
                if self._need(eng, s, v):
                    waits[s] = max(waits.get(s, 0), v)
        if dma:
            assert n == 1
            rr = self.dma_rr.get(eng, 0)
            self.dma_rr[eng] = (rr + 1) % self.DMA_POOL
            sem = f"d_{eng}{rr}"
            prev = self.cnt.get(sem, 0)
            if self.waited[eng].get(sem, 0) < prev:
                waits[sem] = max(waits.get(sem, 0), prev)
            inc = 16
        else:
            sem = self.sem_name(eng, dma)
            inc = n
        self.cnt[sem] = self.cnt.get(sem, 0) + inc
        val = self.cnt[sem]
        for s, v in waits.items():
            self.waited[eng][s] = v
        wl = list(waits.items())

        def cmd(e, wl=wl, fn=fn, sem=sem, inc=inc):
            for s, v in wl:
                e.wait_ge(self.sems[s], v)
            last = fn(e)
            if last is not None:
                last.then_inc(self.sems[sem], inc)

        self.cmds[eng].append(cmd)
        for b in reads:
            self.buf(b).readers.append((sem, val))
        for b in writes:
            bb = self.buf(b)
            bb.w = (sem, val)
            bb.readers = []
        return (sem, val)

    def final_wait(self, eng):
        items = sorted(self.cnt.items())

        def cmd(e, items=items):
            for s, v in items:
                if s != self.sem_name(eng, False):
                    e.wait_ge(self.sems[s], v)
        self.cmds[eng].append(cmd)

    def finalize(self, block):
        for s in sorted(self.cnt):
            g = self.nc.semaphore(s)
            self.sems[s] = g.__enter__()
            self._keep.append(g)
        for ename in ENGINES:
            cmds = self.cmds[ename]
            if not cmds:
                continue

            def body(e, cmds=cmds):
                for c in cmds:
                    c(e)
            getattr(block, ename)(body)


def build_program(depth_a=LA, depth_b=LB, do_boundary=True, do_final=True, dbg=None):
    nc = bass.Bass("TRN2", target_bir_lowering=False, debug=False)
    em = Emit(nc)
    sb = {}
    ps = {}
    dram = {}
    ctxs = []
    GCf = [float(x) for x in host_tables()["gc"]]

    def dram_in(name, shape, dtype=F32):
        dram[name] = nc.dram_tensor(name, list(shape), dtype, kind="ExternalInput").ap()

    def dram_out(name, shape, dtype=F32):
        dram[name] = nc.dram_tensor(name, list(shape), dtype, kind="ExternalOutput").ap()

    def dram_int(name, shape, dtype=F32):
        dram[name] = nc.dram_tensor(name, list(shape), dtype, kind="Internal").ap()

    def sbuf(name, shape, dtype=F32):
        g = nc.sbuf_tensor("s_" + name, list(shape), dtype)
        sb[name] = g.__enter__()
        ctxs.append(g)
        return sb[name]

    def psum(name, shape, dtype=F32):
        g = nc.psum_tensor(name, list(shape), dtype)
        ps[name] = g.__enter__()
        ctxs.append(g)
        return ps[name]

    dram_in("xT", [D, T])
    dram_out("outT", [D, T])
    dram_in("Wq", [LA, D, D]); dram_in("Wk", [LA, D, D]); dram_in("Wv", [LA, D, D])
    dram_in("Wg", [LA, D, D]); dram_in("Wo", [LA, D, D])
    dram_in("W1a", [LA, D, FA]); dram_in("W2a", [LA, FA, D])
    dram_in("ln_sA", [128, 8]); dram_in("ln_bA", [128, 8])
    dram_in("gn_s", [LA, D]); dram_in("gn_b", [LA, D])
    dram_in("cosA", [NTB, 128, 64]); dram_in("sinA", [NTB, 128, 64])
    dram_in("decayD", [HA, CH, CH]); dram_in("gqT", [HA, 128, CH])
    dram_in("gkcol", [CH, HA])
    dram_in("identity", [128, 128]); dram_in("ones", [128, 128])
    dram_in("flags", [128, 4])
    dram_in("Wqkv", [LB, D, 3 * D]); dram_in("WoB", [LB, D, D])
    dram_in("W1b", [LB, D, 2 * DIP]); dram_in("W2b", [LB, DIP, D])
    dram_in("b1b", [2 * DIP // 128, 128, LB])
    dram_in("b2b", [8, 128, LB])
    dram_in("lngB", [8, 128, LB]); dram_in("lnbB", [8, 128, LB])
    dram_in("qscale", [1, 64 * LB]); dram_in("kscale", [1, 64 * LB])
    dram_in("csq", [NTB + 2, 128, 64]); dram_in("ssq", [NTB + 2, 128, 64])
    dram_in("csk", [NTB + 2, 128, 64]); dram_in("ssk", [NTB + 2, 128, 64])
    dram_in("mask_int", [128, 384]); dram_in("mask_e0", [128, 384])
    dram_in("mask_e1", [128, 384])
    dram_int("cc_st_in", [128, HA * 128]); dram_int("cc_st_out", [128, HA * 128])
    dram_int("cc_h_in", [8, 128, HALO]); dram_int("cc_h_out", [8, 128, HALO])
    dram_int("cc_c_in", [128, 8]); dram_int("cc_c_out", [128, 8])
    dram_int("xbin", [D, T])
    dram_int("kTd", [NTB, 128, HA * 128], BF16)

    xTt = sbuf("xT", [128, 8 * XW])
    sbuf("state", [128, HA * 128])
    sbuf("state_bf", [128, HA * 128], BF16)
    sbuf("recv", [128, HA * 128])
    sbuf("cosA", [128, NTB * 64], BF16)
    sbuf("sinA", [128, NTB * 64], BF16)
    sbuf("decayD", [128, HA * 128], BF16)
    sbuf("gqT", [128, HA * 128], BF16)
    sbuf("gkcol", [128, HA])
    sbuf("ident", [128, 128], BF16)
    sbuf("onesb", [128, 128], BF16)
    sbuf("flags", [128, 4])
    sbuf("ln_sA", [128, 8]); sbuf("ln_bA", [128, 8])
    sbuf("gn_srep", [128, D]); sbuf("gn_brep", [128, D])
    sbuf("w0", [128, 8192], BF16)
    sbuf("w1", [128, 8192], BF16)
    sbuf("xn", [128, 8 * 512], BF16)
    sbuf("xsq", [128, 1024], BF16)
    sbuf("brow0", [128, 512]); sbuf("brow1", [128, 512])
    sbuf("bsq", [128, 512])
    sbuf("tscr", [128, 512])
    sbuf("rotst", [128, 4 * D], BF16)
    sbuf("vtok", [128, 4 * D], BF16)
    sbuf("gtok", [128, 4 * D], BF16)
    sbuf("kTrb", [128, 4 * D], BF16)
    sbuf("kgam", [128, D], BF16)
    sbuf("qT", [128, D], BF16); sbuf("qsT", [128, D], BF16)
    sbuf("Adec", [128, HA * CH], BF16)
    sbuf("stat6", [128, 6 * HA])
    sbuf("epscol", [128, 1])
    sbuf("krot", [128, D], BF16)
    sbuf("recip16", [128, 16])
    sbuf("bcol", [128, 12])
    sbuf("mucol", [128, HA]); sbuf("rstdcol", [128, HA]); sbuf("gncol", [128, HA])
    sbuf("ynrm", [128, D])
    sbuf("gyn", [128, D], BF16)
    sbuf("gynT", [128, 8 * 512], BF16)

    PB = [psum(f"pb{i}", [128, 512]) for i in range(7)]
    PT = psum("pt", [128, 1024], BF16)

    def xc(k, t):
        return k * XW + HALO + t

    wslot = [0]

    def load_weight(src_ap, ncols, nchunks):
        s = wslot[0]
        wslot[0] = (s + 1) % 2
        wt = sb[f"w{s}"]
        name = f"wslot{s}"
        assert nchunks * ncols <= 8192
        hn = nchunks // 2
        em.do("gpsimd", lambda e: e.dma_start(
            out=wt[:, 0:hn * ncols].rearrange("p (c n) -> p c n", c=hn),
            in_=src_ap[0:hn * 128, :].rearrange("(c p) n -> c p n", p=128).transpose([1, 0, 2])),
            writes=(name,), dma=True)
        em.do("gpsimd", lambda e: e.dma_start(
            out=wt[:, hn * ncols:nchunks * ncols].rearrange("p (c n) -> p c n", c=hn),
            in_=src_ap[hn * 128:nchunks * 128, :].rearrange("(c p) n -> c p n", p=128).transpose([1, 0, 2])),
            writes=(name,), dma=True)
        return name, wt

    for k in range(8):
        em.do("sync", lambda e, k=k: e.dma_start(
            out=xTt[:, xc(k, 0):xc(k, T)], in_=dram["xT"][k * 128:(k + 1) * 128, :]),
            writes=(f"xT{k}",), dma=True)
    for k in range(8):
        em.do("sync", lambda e, k=k: e.dma_start(
            out=dram["xbin"][k * 128:(k + 1) * 128, :], in_=xTt[:, xc(k, 0):xc(k, T)]),
            reads=(f"xT{k}",), writes=("xbin",), dma=True)
    em.do("vector", lambda e: e.memset(sb["state"][:], 0.0), writes=("state",))
    em.do("vector", lambda e: e.memset(sb["state_bf"][:], 0.0), writes=("state_bf",))
    for k in range(8):
        em.do("vector", lambda e, k=k: e.memset(xTt[:, k * XW:k * XW + HALO], 0.0),
              writes=(f"halo{k}",))
    for nm, dst, src in (
            ("cosA", sb["cosA"].rearrange("p (b f) -> p b f", b=NTB), dram["cosA"].transpose([1, 0, 2])),
            ("sinA", sb["sinA"].rearrange("p (b f) -> p b f", b=NTB), dram["sinA"].transpose([1, 0, 2])),
            ("gkcol", sb["gkcol"][:], dram["gkcol"][:]),
            ("flags", sb["flags"][:], dram["flags"][:]),
            ("ln_sA", sb["ln_sA"][:], dram["ln_sA"][:]),
            ("ln_bA", sb["ln_bA"][:], dram["ln_bA"][:])):
        if nm in ("cosA", "sinA"):
            em.do("gpsimd", lambda e, dst=dst, src=src: e.dma_start(out=dst[:], in_=src),
                  writes=(nm,), dma=True)
        else:
            em.do("sync", lambda e, dst=dst, src=src: e.dma_start(out=dst[:], in_=src),
                  writes=(nm,), dma=True)
    for nm in ("decayD", "gqT"):
        em.do("gpsimd", lambda e, nm=nm: e.dma_start(
            out=sb[nm].rearrange("p (h c) -> p h c", h=HA), in_=dram[nm].transpose([1, 0, 2])),
            writes=(nm,), dma=True)
    em.do("gpsimd", lambda e: e.dma_start(out=sb["ident"][:], in_=dram["identity"][:]),
          writes=("ident",), dma=True)
    em.do("vector", lambda e: e.memset(sb["epscol"][:], EPS), writes=("epscol",))
    em.do("gpsimd", lambda e: e.dma_start(out=sb["onesb"][:], in_=dram["ones"][:]),
          writes=("onesb",), dma=True)

    def ln_fm(st, with_scale, out_bf16=True, shift=False, scol=None, bcol=None):
        t0 = st * 512
        if scol is None:
            scol = lambda k: sb["ln_sA"][:, k:k + 1]
            bcol = lambda k: sb["ln_bA"][:, k:k + 1]

        def xsrc(k):
            off = 1 if (shift and k >= 4) else 0
            return xTt[:, xc(k, t0) - off:xc(k, t0) - off + 512]

        xb = sb["xsq"]
        rdk = (lambda k: (f"xT{k}", f"halo{k}") if (shift and k >= 4) else (f"xT{k}",))
        for k in range(8):
            em.do("vector", lambda e, k=k: e.tensor_copy(
                xb[:, 0:512] if k % 2 == 0 else xb[:, 512:1024], xsrc(k)),
                reads=rdk(k), writes=(f"xsq{k % 2}",))
            em.do("tensor", lambda e, k=k: e.matmul(
                PB[5][:], sb["onesb"][:],
                xb[:, 0:512] if k % 2 == 0 else xb[:, 512:1024],
                start=(k == 0), stop=(k == 7)),
                reads=(f"xsq{k % 2}", "onesb"), writes=("ps5",))
            em.do("vector", lambda e, k=k: e.tensor_mul(
                xb[:, 0:512] if k % 2 == 0 else xb[:, 512:1024],
                xb[:, 0:512] if k % 2 == 0 else xb[:, 512:1024],
                xb[:, 0:512] if k % 2 == 0 else xb[:, 512:1024]),
                reads=(f"xsq{k % 2}",), writes=(f"xsq{k % 2}",))
            em.do("tensor", lambda e, k=k: e.matmul(
                PB[6][:], sb["onesb"][:],
                xb[:, 0:512] if k % 2 == 0 else xb[:, 512:1024],
                start=(k == 0), stop=(k == 7)),
                reads=(f"xsq{k % 2}", "onesb"), writes=("ps6",))
        mu, rr = sb["brow0"], sb["brow1"]
        em.do("vector", lambda e: e.tensor_scalar(out=mu[:], in0=PB[5][:],
                                                  scalar1=1.0 / D, scalar2=None,
                                                  op0=ALU.mult),
              reads=("ps5",), writes=("brow0",))
        em.do("vector", lambda e: e.tensor_scalar(out=rr[:], in0=PB[6][:],
                                                  scalar1=1.0 / D, scalar2=None,
                                                  op0=ALU.mult),
              reads=("ps6",), writes=("brow1",))
        em.do("vector", lambda e: e.tensor_mul(sb["tscr"][:], mu[:], mu[:]),
              reads=("brow0",), writes=("tscr",))
        em.do("vector", lambda e: e.tensor_sub(rr[:], rr[:], sb["tscr"][:]),
              reads=("brow1", "tscr"), writes=("brow1",))
        em.do("scalar", lambda e: e.activation(out=rr[:], in_=rr[:], func=AF.Ln, bias=sb["epscol"][:]),
              reads=("brow1", "epscol"), writes=("brow1",))
        em.do("scalar", lambda e: e.activation(out=rr[:], in_=rr[:], func=AF.Exp,
                                               scale=-0.5),
              reads=("brow1",), writes=("brow1",))
        em.do("vector", lambda e: e.tensor_mul(sb["bsq"][:], mu[:], rr[:]),
              reads=("brow0", "brow1"), writes=("bsq",))
        for k in range(8):
            em.do("vector", lambda e, k=k: e.tensor_mul(
                sb["tscr"][:], xsrc(k), rr[:]),
                reads=rdk(k) + ("brow1",), writes=("tscr",))
            if with_scale:
                em.do("vector", lambda e, k=k: e.tensor_sub(
                    sb["tscr"][:], sb["tscr"][:], sb["bsq"][:]),
                    reads=("tscr", "bsq"), writes=("tscr",))
                em.do("vector", lambda e, k=k: e.tensor_scalar(
                    out=sb["xn"][:, k * 512:(k + 1) * 512], in0=sb["tscr"][:],
                    scalar1=scol(k), scalar2=bcol(k),
                    op0=ALU.mult, op1=ALU.add),
                    reads=("tscr", "ln_sA", "ln_bA", "lngc", "lnbc"), writes=("xn",))
            elif out_bf16:
                em.do("vector", lambda e, k=k: e.tensor_sub(
                    sb["xn"][:, k * 512:(k + 1) * 512], sb["tscr"][:], sb["bsq"][:]),
                    reads=("tscr", "bsq"), writes=("xn",))
            else:
                em.do("vector", lambda e, k=k: e.tensor_sub(
                    xTt[:, xc(k, t0):xc(k, t0) + 512], sb["tscr"][:], sb["bsq"][:]),
                    reads=("tscr", "bsq"), writes=(f"xT{k}",))

    def rotary(j, dst_cols, minus, banks, reads):
        cos = sb["cosA"][:, j * 64:(j + 1) * 64].unsqueeze(1).broadcast_to([128, 4, 64])
        sin = sb["sinA"][:, j * 64:(j + 1) * 64].unsqueeze(1).broadcast_to([128, 4, 64])
        s1 = sb["tscr"]
        s2 = sb["bsq"]
        dst = sb["rotst"]

        def views(b):
            pv = banks[b][:].rearrange("p (h t f) -> p h t f", h=4, t=2)
            return pv[:, :, 0, :], pv[:, :, 1, :]

        def sv(buf, b):
            return buf[:, b * 256:(b + 1) * 256].rearrange("p (h f) -> p h f", h=4)

        tb_i = dst_cols // 1024

        def dslice(b, half):
            v = dst[:].rearrange("p (a h t f) -> p a h t f", a=4, h=8, t=2)
            return v[:, tb_i, 4 * b:4 * b + 4, half, :]

        MUL = ALU.mult
        for b in range(2):
            x1, x2 = views(b)
            em.do("vector", lambda e, b=b, x1=x1: e.tensor_tensor(
                sv(s1, b), x1, cos, MUL), reads=reads + ("cosA",), writes=("tscr",))
            em.do("vector", lambda e, b=b, x2=x2: e.tensor_tensor(
                sv(s2, b), x2, sin, MUL), reads=reads + ("sinA",), writes=("bsq",))
        for b in range(2):
            fn = (lambda e, b=b: e.tensor_tensor(dslice(b, 0), sv(s1, b), sv(s2, b),
                                                 ALU.add if minus else ALU.subtract))
            em.do("vector", fn, reads=("tscr", "bsq"), writes=("rotst",))
        for b in range(2):
            x1, x2 = views(b)
            em.do("vector", lambda e, b=b, x1=x1: e.tensor_tensor(
                sv(s1, b), x1, sin, MUL), reads=reads + ("sinA",), writes=("tscr",))
            em.do("vector", lambda e, b=b, x2=x2: e.tensor_tensor(
                sv(s2, b), x2, cos, MUL), reads=reads + ("cosA",), writes=("bsq",))
        for b in range(2):
            if minus:
                fn = (lambda e, b=b: e.tensor_tensor(dslice(b, 1), sv(s2, b), sv(s1, b),
                                                     ALU.subtract))
            else:
                fn = (lambda e, b=b: e.tensor_tensor(dslice(b, 1), sv(s1, b), sv(s2, b),
                                                     ALU.add))
            em.do("vector", fn, reads=("tscr", "bsq"), writes=("rotst",))

    def proj_st(st, wname, wt, dst_fn, evac, extra_reads=()):
        for tb in range(4):
            def mm(e, tb=tb, wt=wt):
                last = None
                for k in range(8):
                    e.matmul(PB[0][:], sb["xn"][:, k * 512 + tb * 128:k * 512 + (tb + 1) * 128],
                             wt[:, k * 1024:k * 1024 + 512],
                             start=(k == 0), stop=(k == 7))
                for k in range(8):
                    last = e.matmul(PB[1][:],
                                    sb["xn"][:, k * 512 + tb * 128:k * 512 + (tb + 1) * 128],
                                    wt[:, k * 1024 + 512:(k + 1) * 1024],
                                    start=(k == 0), stop=(k == 7))
                return last
            em.do("tensor", mm, reads=("xn", wname), writes=("ps0", "ps1"))
            evac(tb)

    def gn_combine():
        st6 = sb["stat6"]
        mu, rs, t = sb["mucol"], sb["rstdcol"], sb["gncol"]
        em.do("vector", lambda e: e.tensor_add(t[:], st6[:, 1:48:6], st6[:, 4:48:6]),
              reads=("stat6",), writes=("gncol",))
        em.do("vector", lambda e: e.tensor_scalar(out=mu[:], in0=t[:], scalar1=0.5,
                                                  scalar2=None, op0=ALU.mult),
              reads=("gncol",), writes=("mucol",))
        em.do("vector", lambda e: e.tensor_mul(t[:], st6[:, 1:48:6], st6[:, 1:48:6]),
              reads=("stat6",), writes=("gncol",))
        em.do("vector", lambda e: e.tensor_mul(rs[:], st6[:, 4:48:6], st6[:, 4:48:6]),
              reads=("stat6",), writes=("rstdcol",))
        em.do("vector", lambda e: e.tensor_add(t[:], t[:], rs[:]),
              reads=("gncol", "rstdcol"), writes=("gncol",))
        em.do("vector", lambda e: e.tensor_add(rs[:], st6[:, 2:48:6], st6[:, 5:48:6]),
              reads=("stat6",), writes=("rstdcol",))
        em.do("vector", lambda e: e.scalar_tensor_tensor(
            out=t[:], in0=t[:], scalar=64.0, in1=rs[:], op0=ALU.mult, op1=ALU.add),
            reads=("gncol", "rstdcol"), writes=("gncol",))
        em.do("vector", lambda e: e.tensor_scalar(out=t[:], in0=t[:], scalar1=1.0 / 128,
                                                  scalar2=None, op0=ALU.mult),
              reads=("gncol",), writes=("gncol",))
        em.do("vector", lambda e: e.tensor_mul(rs[:], mu[:], mu[:]),
              reads=("mucol",), writes=("rstdcol",))
        em.do("vector", lambda e: e.tensor_sub(t[:], t[:], rs[:]),
              reads=("gncol", "rstdcol"), writes=("gncol",))
        em.do("scalar", lambda e: e.activation(out=t[:], in_=t[:], func=AF.Ln, bias=sb["epscol"][:]),
              reads=("gncol", "epscol"), writes=("gncol",))
        em.do("scalar", lambda e: e.activation(out=rs[:], in_=t[:], func=AF.Exp,
                                               scale=-0.5),
              reads=("gncol",), writes=("rstdcol",))

    def dump_bf16(src_name, width):
        em.do("vector", lambda e: e.tensor_copy(sb["tscr"][:, 0:min(width, 512)],
                                                sb[src_name][:, 0:min(width, 512)]),
              reads=(src_name,), writes=("tscr",))
        em.do("sync", lambda e: e.dma_start(out=dram["outT"][0:128, 0:min(width, 512)],
                                            in_=sb["tscr"][:, 0:min(width, 512)]),
              reads=("tscr",), writes=("outT",), dma=True)

    class DbgStop(Exception):
        pass

    def stage_a_layer(l):
        em.do("sync", lambda e: e.dma_start(
            out=sb["gn_srep"][:],
            in_=dram["gn_s"][l:l + 1, :].partition_broadcast(128)[:, 0, :]),
            writes=("gn_srep",), dma=True)
        em.do("sync", lambda e: e.dma_start(
            out=sb["gn_brep"][:],
            in_=dram["gn_b"][l:l + 1, :].partition_broadcast(128)[:, 0, :]),
            writes=("gn_brep",), dma=True)
        em.do("vector", lambda e: e.memset(sb["state"][:], 0.0), writes=("state",))

        for st in range(NST):
            ln_fm(st, with_scale=True)
            if dbg == "ln1":
                for k in range(8):
                    em.do("vector", lambda e, k=k: e.tensor_copy(
                        sb["tscr"][:], sb["xn"][:, k * 512:(k + 1) * 512]),
                        reads=("xn",), writes=("tscr",))
                    em.do("sync", lambda e, k=k, st=st: e.dma_start(
                        out=dram["outT"][k * 128:(k + 1) * 128, st * 512:(st + 1) * 512],
                        in_=sb["tscr"][:]),
                        reads=("tscr",), writes=("outT",), dma=True)
                continue
            wk_name, wk = load_weight(dram["Wk"][l], 1024, 8)

            def evk(tb, st=st):
                rotary(st * 4 + tb, tb * D, True, (PB[0], PB[1]), ("ps0", "ps1"))
            proj_st(st, wk_name, wk, None, evk)
            wv_name, wv = load_weight(dram["Wv"][l], 1024, 8)

            def evv(tb):
                def f(e, tb=tb):
                    e.copy(sb["vtok"][:, tb * D:tb * D + 512], PB[0][:])
                    return e.copy(sb["vtok"][:, tb * D + 512:(tb + 1) * D], PB[1][:])
                em.do("scalar", f, reads=("ps0", "ps1"), writes=("vtok",))
            proj_st(st, wv_name, wv, None, evv)
            for tb in range(4):
                j = st * 4 + tb
                def kg(e, tb=tb):
                    last = None
                    for h in range(HA):
                        last = e.tensor_scalar(
                            out=sb["kgam"][:, h * 128:(h + 1) * 128],
                            in0=sb["rotst"][:, tb * D + h * 128:tb * D + (h + 1) * 128],
                            scalar1=sb["gkcol"][:, h:h + 1], scalar2=None, op0=ALU.mult)
                    return last
                em.do("vector", kg, reads=("rotst", "gkcol"), writes=("kgam",))
                def trk(e, tb=tb):
                    last = None
                    for c in range(8):
                        last = e.transpose(PT[:, c * 128:(c + 1) * 128],
                                           sb["rotst"][:, tb * D + c * 128:tb * D + (c + 1) * 128],
                                           sb["ident"][:])
                    return last
                em.do("tensor", trk, reads=("rotst", "ident"), writes=("pt",))
                def evt(e, tb=tb):
                    last = None
                    for c in range(8):
                        last = e.tensor_copy(
                            sb["kTrb"][:, tb * D + c * 128:tb * D + (c + 1) * 128],
                            PT[:, c * 128:(c + 1) * 128])
                    return last
                em.do("vector", evt, reads=("pt",), writes=("kTrb",))
                em.do("sync", lambda e, j=j, tb=tb: e.dma_start(
                    out=dram["kTd"][j], in_=sb["kTrb"][:, tb * D:(tb + 1) * D]),
                    reads=("kTrb",), writes=("kTd",), dma=True)
                def scan(e, tb=tb):
                    last = None
                    for h in range(HA):
                        bank = PB[2] if h < 4 else PB[3]
                        last = e.matmul(bank[:, (h % 4) * 128:(h % 4 + 1) * 128],
                                        sb["kgam"][:, h * 128:(h + 1) * 128],
                                        sb["vtok"][:, tb * D + h * 128:tb * D + (h + 1) * 128],
                                        start=True, stop=True)
                    return last
                em.do("tensor", scan, reads=("kgam", "vtok"), writes=("ps2", "ps3"))
                def stup(e):
                    last = None
                    for h in range(HA):
                        bank = PB[2] if h < 4 else PB[3]
                        last = e.scalar_tensor_tensor(
                            out=sb["state"][:, h * 128:(h + 1) * 128],
                            in0=sb["state"][:, h * 128:(h + 1) * 128],
                            scalar=GCf[h], in1=bank[:, (h % 4) * 128:(h % 4 + 1) * 128],
                            op0=ALU.mult, op1=ALU.add)
                    return last
                em.do("vector", stup, reads=("state", "ps2", "ps3"), writes=("state",))

        if dbg in ("ln1x",):
            raise DbgStop()
        if dbg == "ln1":
            raise DbgStop()
        if dbg == "state":
            em.do("sync", lambda e: e.dma_start(out=dram["outT"][0:128, 0:1024],
                                                in_=sb["state"][:]),
                  reads=("state",), writes=("outT",), dma=True)
            raise DbgStop()
        em.do("vector", lambda e: e.tensor_scalar(
            out=sb["recv"][:], in0=sb["state"][:], scalar1=sb["flags"][:, 1:2],
            scalar2=None, op0=ALU.mult),
            reads=("state", "flags"), writes=("recv",))
        em.do("sync", lambda e: e.dma_start(out=dram["cc_st_in"][:], in_=sb["recv"][:]),
              reads=("recv",), writes=("cc_st_in",), dma=True)
        em.do("gpsimd", lambda e: e.collective_compute(
            "AllReduce", ALU.add, ins=[dram["cc_st_in"][:]], outs=[dram["cc_st_out"][:]],
            replica_groups=[[0, 1], [2, 3], [4, 5], [6, 7]]),
            reads=("cc_st_in",), writes=("cc_st_out",))
        em.do("sync", lambda e: e.dma_start(out=sb["recv"][:], in_=dram["cc_st_out"][:]),
              reads=("cc_st_out",), writes=("recv",), dma=True)
        em.do("vector", lambda e: e.tensor_scalar(
            out=sb["state"][:], in0=sb["recv"][:], scalar1=sb["flags"][:, 0:1],
            scalar2=None, op0=ALU.mult),
            reads=("recv", "flags"), writes=("state",))
        em.do("vector", lambda e: e.tensor_copy(sb["state_bf"][:], sb["state"][:]),
              reads=("state",), writes=("state_bf",))
        if dbg == "exch":
            em.do("sync", lambda e: e.dma_start(out=dram["outT"][0:128, 0:1024],
                                                in_=sb["state"][:]),
                  reads=("state",), writes=("outT",), dma=True)
            raise DbgStop()

        for st in range(NST):
            ln_fm(st, with_scale=True)
            wq_name, wq = load_weight(dram["Wq"][l], 1024, 8)

            def evq(tb, st=st):
                rotary(st * 4 + tb, tb * D, False, (PB[0], PB[1]), ("ps0", "ps1"))
            proj_st(st, wq_name, wq, None, evq)
            wv_name, wv = load_weight(dram["Wv"][l], 1024, 8)

            def evv2(tb):
                def f(e, tb=tb):
                    e.copy(sb["vtok"][:, tb * D:tb * D + 512], PB[0][:])
                    return e.copy(sb["vtok"][:, tb * D + 512:(tb + 1) * D], PB[1][:])
                em.do("scalar", f, reads=("ps0", "ps1"), writes=("vtok",))
            proj_st(st, wv_name, wv, None, evv2)
            wg_name, wg = load_weight(dram["Wg"][l], 1024, 8)

            def evg(tb):
                def f(e, tb=tb):
                    e.activation(out=sb["gtok"][:, tb * D:tb * D + 512], in_=PB[0][:],
                                 func=AF.Silu)
                    return e.activation(out=sb["gtok"][:, tb * D + 512:(tb + 1) * D],
                                        in_=PB[1][:], func=AF.Silu)
                em.do("scalar", f, reads=("ps0", "ps1"), writes=("gtok",))
            proj_st(st, wg_name, wg, None, evg)
            if dbg == "p3q" and st == 0:
                em.do("sync", lambda e: e.dma_start(out=dram["outT"][0:128, 0:1024],
                                                    in_=sb["rotst"][:, 0:2048].bitcast(F32)),
                      reads=("rotst",), writes=("outT",), dma=True)
                raise DbgStop()
            em.do("sync", lambda e, st=st: e.dma_start(
                out=sb["kTrb"].rearrange("p (j c) -> p j c", j=4),
                in_=dram["kTd"].transpose([1, 0, 2])[:, st * 4:(st + 1) * 4, :]),
                reads=("kTd",), writes=("kTrb",), dma=True)
            wo_name, wo = load_weight(dram["Wo"][l], 1024, 8)
            for tb in range(4):
                j = st * 4 + tb
                def trq(e, tb=tb):
                    last = None
                    for c in range(8):
                        last = e.transpose(PT[:, c * 128:(c + 1) * 128],
                                           sb["rotst"][:, tb * D + c * 128:tb * D + (c + 1) * 128],
                                           sb["ident"][:])
                    return last
                em.do("tensor", trq, reads=("rotst", "ident"), writes=("pt",))
                def evqT(e):
                    last = None
                    for c in range(8):
                        last = e.tensor_copy(sb["qT"][:, c * 128:(c + 1) * 128],
                                             PT[:, c * 128:(c + 1) * 128])
                    return last
                em.do("vector", evqT, reads=("pt",), writes=("qT",))
                em.do("vector", lambda e: e.tensor_mul(sb["qsT"][:], sb["qT"][:], sb["gqT"][:]),
                      reads=("qT", "gqT"), writes=("qsT",))
                def trkb(e, tb=tb):
                    last = None
                    for c in range(8):
                        last = e.transpose(PT[:, c * 128:(c + 1) * 128],
                                           sb["kTrb"][:, tb * D + c * 128:tb * D + (c + 1) * 128],
                                           sb["ident"][:])
                    return last
                em.do("tensor", trkb, reads=("kTrb", "ident"), writes=("pt",))
                def evkb(e):
                    last = None
                    for h in range(HA):
                        last = e.tensor_scalar(
                            out=sb["kgam"][:, h * 128:(h + 1) * 128],
                            in0=PT[:, h * 128:(h + 1) * 128],
                            scalar1=sb["gkcol"][:, h:h + 1], scalar2=None, op0=ALU.mult)
                    return last
                em.do("vector", evkb, reads=("pt", "gkcol"), writes=("kgam",))
                for hw in range(2):
                    def mm_s(e, hw=hw, tb=tb):
                        last = None
                        for h4 in range(4):
                            h = hw * 4 + h4
                            last = e.matmul(
                                PB[4][:, h4 * 128:(h4 + 1) * 128],
                                sb["kTrb"][:, tb * D + h * 128:tb * D + (h + 1) * 128],
                                sb["qT"][:, h * 128:(h + 1) * 128],
                                start=True, stop=True)
                        return last
                    em.do("tensor", mm_s, reads=("kTrb", "qT"), writes=("ps4",))

                    def ev_s(e, hw=hw):
                        last = None
                        for h4 in range(4):
                            h = hw * 4 + h4
                            last = e.tensor_mul(
                                sb["Adec"][:, h * 128:(h + 1) * 128],
                                PB[4][:, h4 * 128:(h4 + 1) * 128],
                                sb["decayD"][:, h * 128:(h + 1) * 128])
                        return last
                    em.do("vector", ev_s, reads=("ps4", "decayD"), writes=("Adec",))
                if dbg == "adec" and st == 0 and tb == 0:
                    em.do("sync", lambda e: e.dma_start(
                        out=dram["outT"][0:128, 0:512],
                        in_=sb["Adec"][:, 0:1024].bitcast(F32)),
                        reads=("Adec",), writes=("outT",), dma=True)
                    raise DbgStop()
                def mm_y(e, tb=tb):
                    last = None
                    for h in range(HA):
                        bank = PB[5] if h < 4 else PB[6]
                        e.matmul(bank[:, (h % 4) * 128:(h % 4 + 1) * 128],
                                 sb["Adec"][:, h * 128:(h + 1) * 128],
                                 sb["vtok"][:, tb * D + h * 128:tb * D + (h + 1) * 128],
                                 start=True, stop=False)
                        last = e.matmul(bank[:, (h % 4) * 128:(h % 4 + 1) * 128],
                                        sb["qsT"][:, h * 128:(h + 1) * 128],
                                        sb["state_bf"][:, h * 128:(h + 1) * 128],
                                        start=False, stop=True)
                    return last
                em.do("tensor", mm_y, reads=("Adec", "vtok", "qsT", "state_bf"),
                      writes=("ps5", "ps6"))
                if dbg == "ymm" and st == 0 and tb == 0:
                    em.do("vector", lambda e: e.tensor_copy(sb["tscr"][:], PB[5][:]),
                          reads=("ps5",), writes=("tscr",))
                    em.do("sync", lambda e: e.dma_start(out=dram["outT"][0:128, 0:512],
                                                        in_=sb["tscr"][:]),
                          reads=("tscr",), writes=("outT",), dma=True)
                    raise DbgStop()
                def scan2(e, tb=tb):
                    last = None
                    for h in range(HA):
                        bank = PB[2] if h < 4 else PB[3]
                        last = e.matmul(bank[:, (h % 4) * 128:(h % 4 + 1) * 128],
                                        sb["kgam"][:, h * 128:(h + 1) * 128],
                                        sb["vtok"][:, tb * D + h * 128:tb * D + (h + 1) * 128],
                                        start=True, stop=True)
                    return last
                em.do("tensor", scan2, reads=("kgam", "vtok"), writes=("ps2", "ps3"))
                def stup2(e):
                    last = None
                    for h in range(HA):
                        bank = PB[2] if h < 4 else PB[3]
                        last = e.scalar_tensor_tensor(
                            out=sb["state"][:, h * 128:(h + 1) * 128],
                            in0=sb["state"][:, h * 128:(h + 1) * 128],
                            scalar=GCf[h], in1=bank[:, (h % 4) * 128:(h % 4 + 1) * 128],
                            op0=ALU.mult, op1=ALU.add)
                    return last
                em.do("vector", stup2, reads=("state", "ps2", "ps3"), writes=("state",))
                em.do("vector", lambda e: e.tensor_copy(sb["state_bf"][:], sb["state"][:]),
                      reads=("state",), writes=("state_bf",))
                def bnst(e):
                    last = None
                    for h in range(HA):
                        bank = PB[5] if h < 4 else PB[6]
                        last = e.bn_stats(out=sb["stat6"][:, h * 6:(h + 1) * 6],
                                          in_=bank[:, (h % 4) * 128:(h % 4 + 1) * 128])
                    return last
                em.do("vector", bnst, reads=("ps5", "ps6"), writes=("stat6",))
                if dbg == "gnc" and st == 0 and tb == 0:
                    gn_combine()
                    em.do("vector", lambda e: e.tensor_copy(sb["tscr"][:, 0:8], sb["mucol"][:]),
                          reads=("mucol",), writes=("tscr",))
                    em.do("vector", lambda e: e.tensor_copy(sb["tscr"][:, 8:16], sb["rstdcol"][:]),
                          reads=("rstdcol",), writes=("tscr",))
                    em.do("vector", lambda e: e.tensor_copy(sb["tscr"][:, 16:64], sb["stat6"][:]),
                          reads=("stat6",), writes=("tscr",))
                    em.do("sync", lambda e: e.dma_start(out=dram["outT"][0:128, 0:64],
                                                        in_=sb["tscr"][:, 0:64]),
                          reads=("tscr",), writes=("outT",), dma=True)
                    raise DbgStop()
                gn_combine()
                def gnap(e):
                    last = None
                    for h in range(HA):
                        bank = PB[5] if h < 4 else PB[6]
                        last = e.tensor_scalar(
                            out=sb["ynrm"][:, h * 128:(h + 1) * 128],
                            in0=bank[:, (h % 4) * 128:(h % 4 + 1) * 128],
                            scalar1=sb["mucol"][:, h:h + 1],
                            scalar2=sb["rstdcol"][:, h:h + 1],
                            op0=ALU.subtract, op1=ALU.mult)
                    return last
                em.do("vector", gnap, reads=("ps5", "ps6", "mucol", "rstdcol"),
                      writes=("ynrm",))
                if dbg == "ynrm" and st == 0 and tb == 0:
                    em.do("sync", lambda e: e.dma_start(out=dram["outT"][0:128, 0:1024],
                                                        in_=sb["ynrm"][:]),
                          reads=("ynrm",), writes=("outT",), dma=True)
                    raise DbgStop()
                em.do("vector", lambda e: e.tensor_mul(sb["ynrm"][:], sb["ynrm"][:],
                                                       sb["gn_srep"][:]),
                      reads=("ynrm", "gn_srep"), writes=("ynrm",))
                em.do("vector", lambda e: e.tensor_add(sb["ynrm"][:], sb["ynrm"][:],
                                                       sb["gn_brep"][:]),
                      reads=("ynrm", "gn_brep"), writes=("ynrm",))
                em.do("vector", lambda e, tb=tb: e.tensor_mul(
                    sb["gyn"][:], sb["ynrm"][:], sb["gtok"][:, tb * D:(tb + 1) * D]),
                    reads=("ynrm", "gtok"), writes=("gyn",))
                def trg(e):
                    last = None
                    for c in range(8):
                        last = e.transpose(PT[:, c * 128:(c + 1) * 128],
                                           sb["gyn"][:, c * 128:(c + 1) * 128],
                                           sb["ident"][:])
                    return last
                em.do("tensor", trg, reads=("gyn", "ident"), writes=("pt",))
                def evgy(e, tb=tb):
                    last = None
                    for c in range(8):
                        last = e.tensor_copy(
                            sb["gynT"][:, c * 512 + tb * 128:c * 512 + (tb + 1) * 128],
                            PT[:, c * 128:(c + 1) * 128])
                    return last
                em.do("vector", evgy, reads=("pt",), writes=("gynT",))
            for m in range(8):
                def mo(e, m=m):
                    last = None
                    for k in range(8):
                        last = e.matmul(PB[0][:],
                                        wo[:, k * 1024 + m * 128:k * 1024 + (m + 1) * 128],
                                        sb["gynT"][:, k * 512:(k + 1) * 512],
                                        start=(k == 0), stop=(k == 7))
                    return last
                em.do("tensor", mo, reads=("gynT", wo_name), writes=("ps0",))
                em.do("vector", lambda e, m=m, st=st: e.tensor_add(
                    xTt[:, xc(m, st * 512):xc(m, st * 512) + 512], PB[0][:],
                    xTt[:, xc(m, st * 512):xc(m, st * 512) + 512]),
                    reads=("ps0", f"xT{m}"), writes=(f"xT{m}",))

        if dbg == "yr":
            for k in range(8):
                em.do("sync", lambda e, k=k: e.dma_start(
                    out=dram["outT"][k * 128:(k + 1) * 128, :],
                    in_=xTt[:, xc(k, 0):xc(k, T)]),
                    reads=(f"xT{k}",), writes=("outT",), dma=True)
            raise DbgStop()
        for st in range(NST):
            ln_fm(st, with_scale=True)
            for half in range(2):
                wn, wv_ = load_weight(dram["W1a"][l][:, half * 1024:(half + 1) * 1024],
                                      1024, 8)
                for wave in range(2):
                    for m4 in range(4):
                        m = wave * 4 + m4
                        def m1(e, m=m, m4=m4, wv_=wv_):
                            last = None
                            for k in range(8):
                                last = e.matmul(
                                    PB[m4][:],
                                    wv_[:, k * 1024 + m * 128:k * 1024 + (m + 1) * 128],
                                    sb["xn"][:, k * 512:(k + 1) * 512],
                                    start=(k == 0), stop=(k == 7))
                            return last
                        em.do("tensor", m1, reads=("xn", wn), writes=(f"ps{m4}",))
                    def g1(e, wave=wave, half=half):
                        last = None
                        for m4 in range(4):
                            m = half * 8 + wave * 4 + m4
                            buf = sb["rotst"] if m < 8 else sb["vtok"]
                            mm = m if m < 8 else m - 8
                            last = e.activation(out=buf[:, mm * 512:(mm + 1) * 512],
                                                in_=PB[m4][:], func=AF.Gelu_apprx_tanh)
                        return last
                    em.do("scalar", g1, reads=("ps0", "ps1", "ps2", "ps3"),
                          writes=("rotst",) if half == 0 else ("vtok",))
            wn2a, w2a_ = load_weight(dram["W2a"][l][0:1024, :], 1024, 8)
            wn2b, w2b_ = load_weight(dram["W2a"][l][1024:2048, :], 1024, 8)
            for m in range(8):
                def m2(e, m=m, w2a_=w2a_, w2b_=w2b_):
                    last = None
                    bank = PB[4 + (m % 2)]
                    for k in range(8):
                        e.matmul(bank[:],
                                 w2a_[:, k * 1024 + m * 128:k * 1024 + (m + 1) * 128],
                                 sb["rotst"][:, k * 512:(k + 1) * 512],
                                 start=(k == 0), stop=False)
                    for k in range(8):
                        last = e.matmul(
                            bank[:],
                            w2b_[:, k * 1024 + m * 128:k * 1024 + (m + 1) * 128],
                            sb["vtok"][:, k * 512:(k + 1) * 512],
                            start=False, stop=(k == 7))
                    return last
                em.do("tensor", m2, reads=("rotst", "vtok", wn2a, wn2b),
                      writes=(f"ps{4 + (m % 2)}",))
                em.do("vector", lambda e, m=m, st=st: e.tensor_add(
                    xTt[:, xc(m, st * 512):xc(m, st * 512) + 512], PB[4 + (m % 2)][:],
                    xTt[:, xc(m, st * 512):xc(m, st * 512) + 512]),
                    reads=(f"ps{4 + (m % 2)}", f"xT{m}"), writes=(f"xT{m}",))


    GRP = [[0, 1], [2, 3], [4, 5], [6, 7]]
    HEX = sb["state"]

    def halo_exchange():
        for k in range(8):
            em.do("vector", lambda e, k=k: e.tensor_scalar(
                out=HEX[:, 576:834], in0=xTt[:, xc(k, T - 258):xc(k, T)],
                scalar1=sb["flags"][:, 1:2], scalar2=None, op0=ALU.mult),
                reads=(f"xT{k}", "flags"), writes=("state",))
            em.do("sync", lambda e, k=k: e.dma_start(
                out=dram["cc_h_in"][k], in_=HEX[:, 576:834]),
                reads=("state",), writes=("cc_h_in",), dma=True)
        em.do("gpsimd", lambda e: e.collective_compute(
            "AllReduce", ALU.add, ins=[dram["cc_h_in"][:]], outs=[dram["cc_h_out"][:]],
            replica_groups=GRP),
            reads=("cc_h_in",), writes=("cc_h_out",))
        for k in range(8):
            em.do("sync", lambda e, k=k: e.dma_start(
                out=xTt[:, k * XW:k * XW + HALO], in_=dram["cc_h_out"][k]),
                reads=("cc_h_out",), writes=(f"halo{k}",), dma=True)
            em.do("vector", lambda e, k=k: e.tensor_scalar(
                out=xTt[:, k * XW:k * XW + HALO], in0=xTt[:, k * XW:k * XW + HALO],
                scalar1=sb["flags"][:, 0:1], scalar2=None, op0=ALU.mult),
                reads=(f"halo{k}", "flags"), writes=(f"halo{k}",))

    def col_exchange():
        def gat(e):
            last = None
            for k in range(8):
                last = e.tensor_scalar(
                    out=HEX[:, 840 + k:841 + k], in0=xTt[:, xc(k, T - 1):xc(k, T)],
                    scalar1=sb["flags"][:, 1:2], scalar2=None, op0=ALU.mult)
            return last
        em.do("vector", gat, reads=tuple(f"xT{k}" for k in range(8)) + ("flags",),
              writes=("state",))
        em.do("sync", lambda e: e.dma_start(out=dram["cc_c_in"][:], in_=HEX[:, 840:848]),
              reads=("state",), writes=("cc_c_in",), dma=True)
        em.do("gpsimd", lambda e: e.collective_compute(
            "AllReduce", ALU.add, ins=[dram["cc_c_in"][:]], outs=[dram["cc_c_out"][:]],
            replica_groups=GRP),
            reads=("cc_c_in",), writes=("cc_c_out",))
        em.do("sync", lambda e: e.dma_start(out=HEX[:, 840:848], in_=dram["cc_c_out"][:]),
              reads=("cc_c_out",), writes=("state",), dma=True)
        def scat(e):
            last = None
            for k in range(8):
                last = e.tensor_scalar(
                    out=xTt[:, k * XW + 257:k * XW + 258], in0=HEX[:, 840 + k:841 + k],
                    scalar1=sb["flags"][:, 0:1], scalar2=None, op0=ALU.mult)
            return last
        em.do("vector", scat, reads=("state", "flags"),
              writes=tuple(f"halo{k}" for k in range(8)))

    KRING = ["qT", "qsT", "Adec", "kgam", "gyn", "state_bf"]

    def vring_ap(s):
        s = s % 6
        if s < 4:
            return sb["vtok"][:, s * 1024:(s + 1) * 1024]
        return sb["gtok"][:, (s - 4) * 1024:(s - 3) * 1024]

    def vring_tok(s):
        return "vtok" if (s % 6) < 4 else "gtok"

    def btab(name):
        off = {"csqe": 0, "ssqe": 1152, "cske": 2304}.get(name)
        if off is not None:
            return sb["kTrb"][:].bitcast(BF16)[:, off:off + 1152]
        assert name == "sske"
        return sb["state"][:].bitcast(BF16)[:, 0:1152]

    def bmask(i):
        return sb["recv"][:].bitcast(BF16)[:, i * 384:(i + 1) * 384]

    def stage_b_tables(l):
        em.do("sync", lambda e: e.dma_start(
            out=sb["ynrm"][:, 0:64],
            in_=dram["qscale"][0:1, l * 64:(l + 1) * 64].partition_broadcast(128)[:, 0, :]),
            writes=("qsb",), dma=True)
        em.do("sync", lambda e: e.dma_start(
            out=sb["ynrm"][:, 64:128],
            in_=dram["kscale"][0:1, l * 64:(l + 1) * 64].partition_broadcast(128)[:, 0, :]),
            writes=("ksb",), dma=True)
        for nm, raw, scal, tok in (("csqe", "csq", 0, "qsb"), ("ssqe", "ssq", 0, "qsb"),
                                   ("cske", "csk", 64, "ksb"), ("sske", "ssk", 64, "ksb")):
            dstb = btab(nm)
            wtok = "kTrb" if nm != "sske" else "state"
            em.do("gpsimd", lambda e, raw=raw, dstb=dstb: e.dma_start(
                out=dstb.rearrange("p (b f) -> p b f", b=NTB + 2),
                in_=dram[raw].transpose([1, 0, 2])),
                writes=(wtok,), dma=True)
            em.do("vector", lambda e, dstb=dstb, scal=scal: e.tensor_tensor(
                dstb.rearrange("p (b f) -> p b f", b=NTB + 2),
                dstb.rearrange("p (b f) -> p b f", b=NTB + 2),
                sb["ynrm"][:, scal:scal + 64].unsqueeze(1).broadcast_to([128, NTB + 2, 64]),
                ALU.mult),
                reads=(wtok, tok), writes=(wtok,))
        for i, nm in enumerate(("mask_int", "mask_e0", "mask_e1")):
            em.do("gpsimd", lambda e, i=i, nm=nm: e.dma_start(out=bmask(i), in_=dram[nm][:]),
                  writes=("recv",), dma=True)
        em.do("sync", lambda e: e.dma_start(out=sb["ynrm"][:, 128:172],
                                            in_=dram["b1b"][:, :, l].transpose([1, 0]),
                                            allow_slow_non_contiguous=True),
              writes=("b1c",), dma=True)
        em.do("sync", lambda e: e.dma_start(out=sb["ynrm"][:, 176:184],
                                            in_=dram["b2b"][:, :, l].transpose([1, 0]),
                                            allow_slow_non_contiguous=True),
              writes=("b2c",), dma=True)
        em.do("sync", lambda e: e.dma_start(out=sb["ynrm"][:, 184:192],
                                            in_=dram["lngB"][:, :, l].transpose([1, 0]),
                                            allow_slow_non_contiguous=True),
              writes=("lngc",), dma=True)
        em.do("sync", lambda e: e.dma_start(out=sb["ynrm"][:, 192:200],
                                            in_=dram["lnbB"][:, :, l].transpose([1, 0]),
                                            allow_slow_non_contiguous=True),
              writes=("lnbc",), dma=True)

    def xs_tile(kk, t0):
        off = 1 if kk >= 4 else 0
        c0 = kk * XW + HALO + t0 - off
        return xTt[:, c0:c0 + 128]

    def proc_qk(tb, which, kslot):
        qn = sb["xsq"]
        def red(e):
            e.activation(out=qn[:, 0:512], in_=PB[0][:], func=AF.Square)
            return e.activation(out=qn[:, 512:1024], in_=PB[1][:], func=AF.Square)
        em.do("scalar", red, reads=("ps0", "ps1"), writes=("xsq0", "xsq1"))
        em.do("vector", lambda e: e.reduce_sum(
            sb["stat6"][:, 0:16], qn[:].rearrange("p (h f) -> p h f", h=16),
            mybir.AxisListType.X),
            reads=("xsq0", "xsq1"), writes=("st6a",))
        em.do("vector", lambda e: e.tensor_scalar(
            out=sb["stat6"][:, 0:16], in0=sb["stat6"][:, 0:16], scalar1=1e-24,
            scalar2=None, op0=ALU.max),
            reads=("st6a",), writes=("st6a",))
        em.do("scalar", lambda e: e.activation(out=sb["stat6"][:, 16:32],
                                               in_=sb["stat6"][:, 0:16], func=AF.Ln),
              reads=("st6a",), writes=("st6b",))
        em.do("scalar", lambda e: e.activation(out=sb["stat6"][:, 16:32],
                                               in_=sb["stat6"][:, 16:32], func=AF.Exp,
                                               scale=-0.5),
              reads=("st6b",), writes=("st6b",))
        rst = sb["stat6"]
        def nrm(e):
            e.tensor_tensor(qn[:, 0:512].rearrange("p (h f) -> p h f", h=8),
                            PB[0][:].rearrange("p (h f) -> p h f", h=8),
                            rst[:, 16:24].unsqueeze(2).broadcast_to([128, 8, 64]),
                            ALU.mult)
            return e.tensor_tensor(qn[:, 512:1024].rearrange("p (h f) -> p h f", h=8),
                                   PB[1][:].rearrange("p (h f) -> p h f", h=8),
                                   rst[:, 24:32].unsqueeze(2).broadcast_to([128, 8, 64]),
                                   ALU.mult)
        em.do("vector", nrm, reads=("ps0", "ps1", "st6b"), writes=("xsq0", "xsq1"))
        ti = tb + 2
        cs = btab("csqe" if which == "q" else "cske")[:, ti * 64:(ti + 1) * 64]
        ss = btab("ssqe" if which == "q" else "sske")[:, ti * 64:(ti + 1) * 64]
        csb = cs.unsqueeze(1).broadcast_to([128, 16, 64])
        ssb = ss.unsqueeze(1).broadcast_to([128, 16, 64])
        ttoks = ("kTrb", "state")
        t1 = sb["krot"]
        t2b = sb["sinA"]
        qv = qn[:].rearrange("p (h f) -> p h f", h=16)
        qsv = qn[:].rearrange("p (h t f) -> p h t f", h=16, t=2)
        em.do("vector", lambda e: e.tensor_tensor(
            t1[:].rearrange("p (h f) -> p h f", h=16), qv, csb, ALU.mult),
            reads=("xsq0", "xsq1") + ttoks, writes=("krot",))
        def swp(e):
            v2 = t2b[:].rearrange("p (h t f) -> p h t f", h=16, t=2)
            ssv = ssb.rearrange("p h (t f) -> p h t f", t=2)
            e.tensor_tensor(v2[:, :, 0, :], qsv[:, :, 1, :], ssv[:, :, 0, :], ALU.mult)
            return e.tensor_tensor(v2[:, :, 1, :], qsv[:, :, 0, :], ssv[:, :, 1, :],
                                   ALU.mult)
        em.do("vector", swp, reads=("xsq0", "xsq1") + ttoks, writes=("sinA",))
        if which == "q":
            dst = sb["rotst"][:, (tb % 4) * 1024:(tb % 4 + 1) * 1024]
            em.do("vector", lambda e, dst=dst: e.tensor_add(dst, t1[:], t2b[:]),
                  reads=("krot", "sinA"), writes=("rotst",))
        else:
            em.do("vector", lambda e: e.tensor_add(t1[:], t1[:], t2b[:]),
                  reads=("krot", "sinA"), writes=("krot",))
            kt = sb[kslot]
            def trk(e):
                last = None
                for c in range(8):
                    last = e.transpose(PT[:, c * 128:(c + 1) * 128],
                                       t1[:, c * 128:(c + 1) * 128], sb["ident"][:])
                return last
            em.do("tensor", trk, reads=("krot", "ident"), writes=("pt",))
            def evk(e, kt=kt):
                last = None
                for c in range(8):
                    last = e.tensor_copy(kt[:, c * 128:(c + 1) * 128],
                                         PT[:, c * 128:(c + 1) * 128])
                return last
            em.do("vector", evk, reads=("pt",), writes=(kslot,))

    def qT_stage(qb):
        src = sb["rotst"][:, (qb % 4) * 1024:(qb % 4 + 1) * 1024]
        def trq(e, src=src):
            last = None
            for c in range(8):
                last = e.transpose(PT[:, c * 128:(c + 1) * 128],
                                   src[:, c * 128:(c + 1) * 128], sb["ident"][:])
            return last
        em.do("tensor", trq, reads=("rotst", "ident"), writes=("pt",))
        def evq(e):
            last = None
            for c in range(8):
                last = e.tensor_copy(sb["krot"][:, c * 128:(c + 1) * 128],
                                     PT[:, c * 128:(c + 1) * 128])
            return last
        em.do("vector", evq, reads=("pt",), writes=("krot",))

    def attn_scores(qb):
        qT_stage(qb)
        key_tbs = [qb - 2, qb - 1, qb]
        mk = bmask(1) if qb == 0 else (bmask(2) if qb == 1 else bmask(0))
        for h in range(HB):
            pair = h // 2
            base = 64 * (h % 2)
            def mm_s(e, h=h, pair=pair, base=base):
                last = None
                for ci, ktb in enumerate(key_tbs):
                    ks = sb[KRING[(ktb + 2) % 6]]
                    last = e.matmul(
                        PB[4][:, ci * 128:(ci + 1) * 128],
                        sb["krot"][base:base + 64, pair * 128:(pair + 1) * 128],
                        ks[base:base + 64, pair * 128:(pair + 1) * 128],
                        start=(ci == 0), stop=(ci == 2))
                return last
            em.do("tensor", mm_s,
                  reads=("krot",) + tuple(KRING[(ktb + 2) % 6] for ktb in key_tbs),
                  writes=("ps4",))
            em.do("vector", lambda e, mk=mk: e.tensor_tensor(
                sb["tscr"][:, 0:384], PB[4][:, 0:384], mk, ALU.add),
                reads=("ps4", "recv"), writes=("tscr",))
            em.do("vector", lambda e, h=h: e.reduce_max(
                sb["stat6"][:, 32 + h:33 + h], sb["tscr"][:, 0:384],
                mybir.AxisListType.X, negate=True),
                reads=("tscr",), writes=("st6c",))
            aslot = sb["decayD"][:, (h % 2) * 384:(h % 2) * 384 + 384]
            em.do("scalar", lambda e, h=h, aslot=aslot: e.activation(
                out=aslot, in_=sb["tscr"][:, 0:384], func=AF.Exp,
                bias=sb["stat6"][:, 32 + h:33 + h],
                accum_out=sb["stat6"][:, h:h + 1]),
                reads=("tscr", "st6c"), writes=("decayD", "st6a"))
            em.do("vector", lambda e, h=h: e.reciprocal(
                sb["recip16"][:, h:h + 1], sb["stat6"][:, h:h + 1]),
                reads=("st6a",), writes=("recip",))
            em.do("vector", lambda e, h=h, aslot=aslot: e.tensor_scalar(
                out=aslot, in0=aslot, scalar1=sb["recip16"][:, h:h + 1],
                scalar2=None, op0=ALU.mult),
                reads=("recip", "decayD"), writes=("decayD",))
            atp = sb["gqT"][:, (h % 2) * 384:(h % 2) * 384 + 384]
            def trA(e, aslot=aslot):
                last = None
                for c in range(3):
                    last = e.transpose(PT[:, c * 128:(c + 1) * 128],
                                       aslot[:, c * 128:(c + 1) * 128], sb["ident"][:])
                return last
            em.do("tensor", trA, reads=("decayD", "ident"), writes=("pt",))
            def evA(e, atp=atp):
                last = None
                for c in range(3):
                    last = e.tensor_copy(atp[:, c * 128:(c + 1) * 128],
                                         PT[:, c * 128:(c + 1) * 128])
                return last
            em.do("vector", evA, reads=("pt",), writes=("gqT",))
            ybank = PB[5] if pair < 4 else PB[6]
            def mm_av(e, h=h, atp=atp, ybank=ybank, base=base, pair=pair):
                last = None
                for ci, ktb in enumerate(key_tbs):
                    vs = vring_ap(ktb + 2)
                    last = e.matmul(
                        ybank[base:base + 128 - 64, (pair % 4) * 128:(pair % 4 + 1) * 128]
                        if False else
                        ybank[base:base + 64, (pair % 4) * 128:(pair % 4 + 1) * 128],
                        vs[:, h * 64:(h + 1) * 64],
                        atp[:, ci * 128:(ci + 1) * 128],
                        start=(ci == 0), stop=(ci == 2),
                        tile_position=(0, base))
                return last
            vtoks = tuple(vring_tok(ktb + 2) for ktb in key_tbs)
            em.do("tensor", mm_av, reads=("gqT",) + vtoks,
                  writes=("ps5",) if pair < 4 else ("ps6",))
        def evy(e, qb=qb):
            last = None
            for pair in range(8):
                ybank = PB[5] if pair < 4 else PB[6]
                last = e.tensor_copy(
                    sb["gynT"][:, pair * 512 + (qb % 4) * 128:
                               pair * 512 + (qb % 4) * 128 + 128],
                    ybank[:, (pair % 4) * 128:(pair % 4 + 1) * 128])
            return last
        em.do("vector", evy, reads=("ps5", "ps6"), writes=("gynT",))

    def wo_st(l, st):
        wn, wo = load_weight(dram["WoB"][l], 1024, 8)
        for m in range(8):
            def mo(e, m=m, wo=wo):
                last = None
                for k in range(8):
                    last = e.matmul(PB[0][:],
                                    wo[:, k * 1024 + m * 128:k * 1024 + (m + 1) * 128],
                                    sb["gynT"][:, k * 512:(k + 1) * 512],
                                    start=(k == 0), stop=(k == 7))
                return last
            em.do("tensor", mo, reads=("gynT", wn), writes=("ps0",))
            em.do("vector", lambda e, m=m, st=st: e.tensor_add(
                xTt[:, xc(m, st * 512):xc(m, st * 512) + 512], PB[0][:],
                xTt[:, xc(m, st * 512):xc(m, st * 512) + 512]),
                reads=("ps0", f"xT{m}"), writes=(f"xT{m}",))

    def ffn_b(l):
        for st in reversed(range(NST)):
            ln_fm(st, with_scale=True, shift=True,
                  scol=lambda k: sb["ynrm"][:, 184 + k:185 + k],
                  bcol=lambda k: sb["ynrm"][:, 192 + k:193 + k])
            units = [list(range(u * 4, min(u * 4 + 4, 22))) for u in range(6)]
            for u, chunks in enumerate(units):
                nc_ = len(chunks)
                wna, wa = load_weight(
                    dram["W1b"][l][:, chunks[0] * 128:chunks[0] * 128 + nc_ * 128],
                    nc_ * 128, 8)
                wng, wg_ = load_weight(
                    dram["W1b"][l][:, DIP + chunks[0] * 128:DIP + chunks[0] * 128 + nc_ * 128],
                    nc_ * 128, 8)
                em.do("gpsimd", lambda e, chunks=chunks, nc_=nc_: e.dma_start(
                    out=sb["gynT"][:, 0:nc_ * 1024].rearrange("p (c n) -> p c n", c=nc_),
                    in_=dram["W2b"][l][chunks[0] * 128:chunks[0] * 128 + nc_ * 128].rearrange(
                        "(c p) n -> c p n", p=128).transpose([1, 0, 2])),
                    writes=("gynT",), dma=True)
                for ci in range(nc_):
                    ch = chunks[ci]
                    def m1(e, ci=ci, nc_=nc_, st=st, wa=wa, wg_=wg_):
                        last = None
                        for k in range(8):
                            xns = sb["xn"][:, k * 512:(k + 1) * 512]
                            e.matmul(PB[0][:],
                                     wa[:, k * nc_ * 128 + ci * 128:
                                        k * nc_ * 128 + (ci + 1) * 128],
                                     xns, start=(k == 0), stop=(k == 7))
                        for k in range(8):
                            xns = sb["xn"][:, k * 512:(k + 1) * 512]
                            last = e.matmul(PB[1][:],
                                            wg_[:, k * nc_ * 128 + ci * 128:
                                                k * nc_ * 128 + (ci + 1) * 128],
                                            xns, start=(k == 0), stop=(k == 7))
                        return last
                    em.do("tensor", m1, reads=("xn", wna, wng), writes=("ps0", "ps1"))
                    em.do("scalar", lambda e, ch=ch: e.activation(
                        out=sb["krot"][:, 0:512], in_=PB[1][:], func=AF.Gelu,
                        bias=sb["ynrm"][:, 150 + ch:151 + ch]),
                        reads=("ps1", "b1c"), writes=("krot",))
                    em.do("vector", lambda e, ch=ch: e.tensor_scalar(
                        out=sb["sinA"][:, 0:512], in0=PB[0][:],
                        scalar1=sb["ynrm"][:, 128 + ch:129 + ch], scalar2=None,
                        op0=ALU.add),
                        reads=("ps0", "b1c"), writes=("sinA",))
                    hbuf = sb["xsq"] if ci < 2 else sb["cosA"]
                    htok = (f"xsq{ci % 2}",) if ci < 2 else ("cosA",)
                    em.do("vector", lambda e, ci=ci, hbuf=hbuf: e.tensor_mul(
                        hbuf[:, (ci % 2) * 512:(ci % 2 + 1) * 512],
                        sb["sinA"][:, 0:512], sb["krot"][:, 0:512]),
                        reads=("sinA", "krot"), writes=htok)
                for p2_ in range(2):
                    for ci in range(nc_):
                        hbuf = sb["xsq"] if ci < 2 else sb["cosA"]
                        htok = (f"xsq{ci % 2}",) if ci < 2 else ("cosA",)
                        def m2(e, ci=ci, nc_=nc_, p2_=p2_, hbuf=hbuf):
                            last = None
                            for m4 in range(4):
                                mo = p2_ * 4 + m4
                                last = e.matmul(
                                    PB[2 + m4][:],
                                    sb["gynT"][:, ci * 1024 + mo * 128:
                                               ci * 1024 + (mo + 1) * 128],
                                    hbuf[:, (ci % 2) * 512:(ci % 2 + 1) * 512],
                                    start=(ci == 0), stop=(ci == nc_ - 1))
                            return last
                        em.do("tensor", m2, reads=("gynT",) + htok,
                              writes=("ps2", "ps3", "ps4", "ps5"))
                    for m4 in range(4):
                        mo = p2_ * 4 + m4
                        def ev(e, mo=mo, m4=m4, st=st, u=u):
                            dst = xTt[:, xc(mo, st * 512):xc(mo, st * 512) + 512]
                            if u == 5:
                                return e.scalar_tensor_tensor(
                                    out=dst, in0=PB[2 + m4][:],
                                    scalar=sb["ynrm"][:, 176 + mo:177 + mo],
                                    in1=dst, op0=ALU.add, op1=ALU.add)
                            return e.tensor_add(dst, PB[2 + m4][:], dst)
                        em.do("vector", ev,
                              reads=(f"ps{2 + m4}", f"xT{mo}", "b2c"),
                              writes=(f"xT{mo}",))

    def snapshot_bcols():
        def snap(e):
            last = None
            for k in range(4, 8):
                for j in range(3):
                    last = e.tensor_copy(
                        sb["bcol"][:, (k - 4) * 3 + j:(k - 4) * 3 + j + 1],
                        xTt[:, xc(k, (j + 1) * 512 - 1):xc(k, (j + 1) * 512 - 1) + 1])
            return last
        em.do("vector", snap, reads=tuple(f"xT{k}" for k in range(4, 8)),
              writes=("bcol",))

    def stage_b_layer(l):
        halo_exchange()
        snapshot_bcols()
        stage_b_tables(l)
        st_groups = [(-2, -1), (0, 1, 2, 3), (4, 5, 6, 7), (8, 9, 10, 11),
                     (12, 13, 14, 15)]
        for gi, tbs in enumerate(st_groups):
            halo_grp = (gi == 0)
            for ti, tb in enumerate(tbs):
                bfix = (ti == 0 and gi >= 2)
                def cst(e, tb=tb, ti=ti, bfix=bfix, gi=gi):
                    last = None
                    for k in range(8):
                        dst = sb["gynT"][:, ti * 1024 + k * 128:ti * 1024 + (k + 1) * 128]
                        if bfix and k >= 4:
                            e.tensor_copy(dst[:, 0:1],
                                          sb["bcol"][:, (k - 4) * 3 + gi - 2:
                                                     (k - 4) * 3 + gi - 1])
                            last = e.tensor_copy(dst[:, 1:128],
                                                 xs_tile(k, tb * 128)[:, 1:128])
                        else:
                            last = e.tensor_copy(dst, xs_tile(k, tb * 128))
                    return last
                em.do("vector", cst,
                      reads=tuple(f"xT{k}" for k in range(8)) +
                            tuple(f"halo{k}" for k in range(8)) + ("bcol",),
                      writes=("gynT",))
            passes = (("k", 1024), ("v", 2048)) if halo_grp else                      (("q", 0), ("k", 1024), ("v", 2048))
            for which, col0 in passes:
                wn, wt = load_weight(dram["Wqkv"][l][:, col0:col0 + 1024], 1024, 8)
                for ti, tb in enumerate(tbs):
                    def mm(e, ti=ti, wt=wt):
                        last = None
                        for half in range(2):
                            for k in range(8):
                                last = e.matmul(
                                    PB[half][:],
                                    sb["gynT"][:, ti * 1024 + k * 128:
                                               ti * 1024 + (k + 1) * 128],
                                    wt[:, k * 1024 + half * 512:
                                       k * 1024 + (half + 1) * 512],
                                    start=(k == 0), stop=(k == 7))
                        return last
                    em.do("tensor", mm, reads=("gynT", wn), writes=("ps0", "ps1"))
                    if which == "v":
                        def evv(e, tb=tb):
                            e.copy(vring_ap(tb + 2)[:, 0:512], PB[0][:])
                            return e.copy(vring_ap(tb + 2)[:, 512:1024], PB[1][:])
                        em.do("scalar", evv, reads=("ps0", "ps1"),
                              writes=(vring_tok(tb + 2),))
                    else:
                        proc_qk(tb, which, KRING[(tb + 2) % 6])
            if not halo_grp:
                for qb in tbs:
                    attn_scores(qb)
                wo_st(l, tbs[0] // 4)
        if dbg == "battn":
            for k in range(8):
                em.do("sync", lambda e, k=k: e.dma_start(
                    out=dram["outT"][k * 128:(k + 1) * 128, :],
                    in_=xTt[:, xc(k, 0):xc(k, T)]),
                    reads=(f"xT{k}",), writes=("outT",), dma=True)
            raise DbgStop()
        col_exchange()
        ffn_b(l)

    def boundary():
        for st in range(NST):
            for k in range(8):
                em.do("sync", lambda e, k=k, st=st: e.dma_start(
                    out=sb["recv"][:, 0:512],
                    in_=dram["xbin"][k * 128:(k + 1) * 128, st * 512:(st + 1) * 512]),
                    reads=(), writes=("recv",), dma=True)
                em.do("vector", lambda e, k=k, st=st: e.tensor_add(
                    xTt[:, xc(k, st * 512):xc(k, st * 512) + 512],
                    xTt[:, xc(k, st * 512):xc(k, st * 512) + 512],
                    sb["recv"][:, 0:512]),
                    reads=("recv", f"xT{k}"), writes=(f"xT{k}",))
            ln_fm(st, with_scale=False, out_bf16=False)

    try:
        for l in range(depth_a):
            stage_a_layer(l)
        dbg_hit = False
    except DbgStop:
        dbg_hit = True
    if do_boundary and not dbg_hit:
        boundary()
        if depth_b > 0:
            for k in range(8):
                em.do("sync", lambda e, k=k: e.dma_start(
                    out=dram["xbin"][k * 128:(k + 1) * 128, :],
                    in_=xTt[:, xc(k, 0):xc(k, T)]),
                    reads=(f"xT{k}",), writes=("xbin",), dma=True)
    if not dbg_hit:
        try:
            for l in range(depth_b):
                stage_b_layer(l)
            if depth_b > 0 and do_final:
                boundary()
        except DbgStop:
            dbg_hit = True
    if not dbg_hit:
        for k in range(8):
            em.do("sync", lambda e, k=k: e.dma_start(
                out=dram["outT"][k * 128:(k + 1) * 128, :], in_=xTt[:, xc(k, 0):xc(k, T)]),
                reads=(f"xT{k}",), writes=("outT",), dma=True)
    em.final_wait("sync")
    with nc.Block() as blk:
        em.finalize(blk)
    nc._keep_ctxs = ctxs
    return nc


def prep_inputs(inputs):
    t = host_tables()
    perm = np.empty(DHA, np.int64)
    perm[:64] = np.arange(0, DHA, 2)
    perm[64:] = np.arange(1, DHA, 2)
    colperm = np.concatenate([h * DHA + perm for h in range(HA)])
    Wq = np.ascontiguousarray(inputs["L0_attn_Wq"][:, :, colperm])
    Wk = np.ascontiguousarray(inputs["L0_attn_Wk"][:, :, colperm])
    W1b_raw = inputs["L0_ffn_W1"]
    b1_raw = inputs["L0_ffn_b1"]
    W1p = np.zeros((LB, D, 2 * DIP), np.float32)
    b1p = np.zeros((LB, 2 * DIP), np.float32)
    W1p[:, :, :DI] = W1b_raw[:, :, :DI]
    W1p[:, :, DIP:DIP + DI] = W1b_raw[:, :, DI:]
    b1p[:, :DI] = b1_raw[:, :DI]
    b1p[:, DIP:DIP + DI] = b1_raw[:, DI:]
    W2p = np.zeros((LB, DIP, D), np.float32)
    W2p[:, :DI, :] = inputs["L0_ffn_W2"]
    common = {
        "Wq": Wq, "Wk": Wk,
        "Wv": np.ascontiguousarray(inputs["L0_attn_Wv"]),
        "Wg": np.ascontiguousarray(inputs["L0_attn_Wg"]),
        "Wo": np.ascontiguousarray(inputs["L0_attn_Wo"]),
        "W1a": np.ascontiguousarray(inputs["L0_attn_W1"]),
        "W2a": np.ascontiguousarray(inputs["L0_attn_W2"]),
        "ln_sA": np.ascontiguousarray(inputs["L0_attn_ln_scale"].reshape(8, 128).T),
        "ln_bA": np.ascontiguousarray(inputs["L0_attn_ln_bias"].reshape(8, 128).T),
        "gn_s": np.ascontiguousarray(inputs["L0_attn_gn_scale"]),
        "gn_b": np.ascontiguousarray(inputs["L0_attn_gn_bias"]),
        "decayD": t["decayD"], "gqT": t["gqT"], "gkcol": t["gkcol"],
        "identity": t["identity"], "ones": t["ones"],
        "Wqkv": np.ascontiguousarray(inputs["L0_ffn_Wqkv"]),
        "WoB": np.ascontiguousarray(inputs["L0_ffn_Wo"]),
        "W1b": W1p,
        "b1b": np.ascontiguousarray(b1p.reshape(LB, 2 * DIP // 128, 128).transpose(1, 2, 0)),
        "W2b": W2p,
        "b2b": np.ascontiguousarray(inputs["L0_ffn_b2"].reshape(LB, 8, 128).transpose(1, 2, 0)),
        "lngB": np.ascontiguousarray(inputs["L0_ffn_ln_g"].reshape(LB, 8, 128).transpose(1, 2, 0)),
        "lnbB": np.ascontiguousarray(inputs["L0_ffn_ln_b"].reshape(LB, 8, 128).transpose(1, 2, 0)),
        "qscale": np.ascontiguousarray(inputs["L0_ffn_q_scale"].reshape(1, LB * 64)),
        "kscale": np.ascontiguousarray(inputs["L0_ffn_k_scale"].reshape(1, LB * 64)),
        "mask_int": stage_b_masks(0)["mask_int"],
    }
    X = inputs["L0_attn_X"]
    in_maps = []
    for c in range(8):
        b, tok0 = c // 2, (c % 2) * T
        m = dict(common)
        m["xT"] = np.ascontiguousarray(X[b, tok0:tok0 + T, :].T)
        m["cosA"] = np.ascontiguousarray(t["cosA"][tok0:tok0 + T].reshape(NTB, 128, 64))
        m["sinA"] = np.ascontiguousarray(t["sinA"][tok0:tok0 + T].reshape(NTB, 128, 64))
        lo = tok0 - 256

        def sl(tab, lo=lo, tok0=tok0):
            out = np.zeros(((NTB + 2) * 128, 64), np.float32)
            src_lo = max(lo, 0)
            pad = src_lo - lo
            seg = tab[src_lo:tok0 + T]
            out[pad:pad + seg.shape[0]] = seg
            return np.ascontiguousarray(out.reshape(NTB + 2, 128, 64))
        m["csq"] = sl(t["csq"]); m["ssq"] = sl(t["ssq"])
        m["csk"] = sl(t["csk"]); m["ssk"] = sl(t["ssk"])
        em_ = stage_b_masks(c)
        m["mask_e0"] = em_["mask_e0"]; m["mask_e1"] = em_["mask_e1"]
        fl = np.zeros((128, 4), np.float32)
        fl[:, 0] = 1.0 if c % 2 == 1 else 0.0
        fl[:, 1] = 1.0 if c % 2 == 0 else 0.0
        m["flags"] = fl
        in_maps.append(m)
    return in_maps


_BUILD_KW = {}


def kernel(**inputs):
    in_maps = prep_inputs(inputs)
    nc = build_program(**_BUILD_KW)
    res = run_bass_kernel_spmd(nc, in_maps, core_ids=list(range(8)))
    out = np.empty((B, S, D), np.float32)
    for c in range(8):
        b, tok0 = c // 2, (c % 2) * T
        out[b, tok0:tok0 + T, :] = np.asarray(res.results[c]["outT"]).T
    return out
